# version 02 (final)
# baseline (speedup 1.0000x reference)
import numpy as np
import ml_dtypes
from contextlib import ExitStack
import concourse.bass as bass
import concourse.mybir as mybir
from concourse.bass_utils import run_bass_kernel_spmd

F32 = mybir.dt.float32
BF16 = mybir.dt.bfloat16
AF = mybir.ActivationFunctionType
ALU = mybir.AluOpType
AX = mybir.AxisListType

NB = 2
T = 4096
D = 1024
S2 = 1.0 / 1024.0
NEG = -1e30
EPS = 1e-5


def host_constants():
    ident = np.eye(128, dtype=np.float32)
    identbf = ident.astype(ml_dtypes.bfloat16)
    dh = 64
    half = np.arange(0, dh, 2, dtype=np.float64)
    inv_freq = 1.0 / 10000.0 ** (half / dh)
    t = np.arange(T, dtype=np.float64)
    freqs = t[:, None] * inv_freq[None, :]
    freqs = np.concatenate([freqs, freqs], axis=1)
    base = (half + 0.4 * dh) / (1.4 * dh)
    power = (t - T // 2) / 256.0
    scale = base[None, :] ** power[:, None]
    scale = np.concatenate([scale, scale], axis=1)
    cs = np.cos(freqs); sn = np.sin(freqs)
    rot = np.concatenate([cs * scale * 8.0, sn * scale * 8.0,
                          cs / scale, sn / scale], axis=1).astype(np.float32)
    ii = np.arange(128)[:, None]; jj = np.arange(128)[None, :]
    mask_first = np.where(jj >= ii, 0.0, NEG).astype(np.float32)
    mask_last = np.where(jj <= ii, 0.0, NEG).astype(np.float32)
    return {"c_ident": ident, "c_identbf": identbf, "c_rot": rot,
            "c_mfirst": mask_first, "c_mlast": mask_last}

ENGINES = ("sync", "tensor", "vector", "scalar", "gpsimd")

class Prog:
    def __init__(self, nc, stk):
        self.nc = nc
        self.stk = stk
        self.items = {e: [] for e in ENGINES}
        self.sems = {e: stk.enter_context(nc.semaphore(f"sem_{e}")) for e in ENGINES}
        self.cnt = {e: 0 for e in ENGINES}
        self.waited = {e: {} for e in ENGINES}

    def stream(self, eng, name, depth):
        return Stream(self, eng, name, depth)

    def op(self, eng, fn, inc=1):
        sem = self.sems[eng]
        self.cnt[eng] += inc
        def run(h, fn=fn, sem=sem, inc=inc):
            fn(h).then_inc(sem, inc)
        self.items[eng].append(run)
        return (eng, self.cnt[eng])

    def wait(self, eng, marker):
        if marker is None:
            return
        key, val = marker
        assert val <= self.cnt[key], f"wait on future {key} op ({val} > {self.cnt[key]})"
        w = self.waited[eng]
        if w.get(key, -1) >= val:
            return
        w[key] = val
        sem = self.sems[key]
        def run(h, sem=sem, val=val):
            h.wait_ge(sem, val)
        self.items[eng].append(run)

    def wait_all(self, eng, markers):
        best = {}
        for m in markers or []:
            if m is None:
                continue
            k, v = m
            if best.get(k, -1) < v:
                best[k] = v
        for k, v in best.items():
            self.wait(eng, (k, v))

    def mark(self, eng):
        return (eng, self.cnt[eng])

    def barrier(self):
        marks = [self.mark(e) for e in ENGINES]
        for e in ENGINES:
            self.wait_all(e, [m for m in marks if m[0] != e])

    def emit(self, block):
        for eng in ENGINES:
            items = self.items[eng]
            getattr(block, eng)(lambda h, items=items: [it(h) for it in items])


class Stream:
    def __init__(self, P, eng, name, depth):
        self.P = P
        self.eng = eng
        self.depth = depth
        self.keys = []
        for i in range(depth):
            key = f"{name}{i}"
            P.sems[key] = P.stk.enter_context(P.nc.semaphore(key))
            P.cnt[key] = 0
            self.keys.append(key)
        self.i = 0

    def dma(self, out, in_, wait=(), **kw):
        P = self.P
        k = self.keys[self.i % self.depth]
        self.i += 1
        prev = P.cnt[k]
        if prev:
            w = P.waited[self.eng]
            if w.get(k, -1) < prev:
                w[k] = prev
                sem = P.sems[k]
                def runw(h, sem=sem, val=prev):
                    h.wait_ge(sem, val)
                P.items[self.eng].append(runw)
        P.wait_all(self.eng, wait)
        sem = P.sems[k]
        P.cnt[k] += 16
        def run(h, sem=sem, out=out, in_=in_, kw=kw):
            h.dma_start(out=out, in_=in_, **kw).then_inc(sem, 16)
        P.items[self.eng].append(run)
        return (k, P.cnt[k])


class LNHelper:
    def __init__(self, nc, P, stk, name, jnk):
        self.P = P
        self.stat = stk.enter_context(nc.sbuf_tensor(f"{name}_stat", [128, 12], F32))
        self.jnk = jnk
        self.prev_apply = None
        self.par = 0
        self.halfc = stk.enter_context(nc.sbuf_tensor(f"{name}_half", [128, 1], F32))
        P.op("gpsimd", lambda h: h.memset(self.halfc[:], 0.5))
        self.init_m = P.op("gpsimd", lambda h: h.memset(self.stat[:], 0.0))

    def run(self, xsb, outs, dep=None):
        P = self.P
        o = 6 * self.par
        self.par ^= 1
        st = self.stat
        mu = st[:, o:o + 1]; ssq = st[:, o + 1:o + 2]; var = st[:, o + 2:o + 3]
        rstd = st[:, o + 3:o + 4]; mrs = st[:, o + 4:o + 5]
        P.wait("vector", self.prev_apply)
        P.wait("vector", self.init_m)
        P.wait("scalar", self.init_m)
        P.wait_all("vector", [dep])
        P.wait_all("scalar", [dep])
        m1 = P.op("vector", lambda h: h.tensor_reduce(out=mu, in_=xsb[:], axis=AX.X, op=ALU.add))
        P.wait("scalar", self.prev_apply)
        sq = P.op("scalar", lambda h: h.activation(out=self.jnk[:], in_=xsb[:], func=AF.Square, accum_out=ssq))
        fl = P.op("scalar", lambda h: h.activation(out=st[:, o + 5:o + 6], in_=st[:, o + 5:o + 6],
                                                   func=AF.Identity))
        P.wait("vector", fl)
        P.wait("vector", m1)
        m2 = P.op("vector", lambda h: h.tensor_scalar(out=mu, in0=mu, scalar1=1.0 / D, scalar2=None, op0=ALU.mult))
        P.wait("vector", m2)
        m3 = P.op("vector", lambda h: h.tensor_tensor(out=var, in0=mu, in1=mu, op=ALU.mult))
        P.wait("vector", m3)
        m4 = P.op("vector", lambda h: h.scalar_tensor_tensor(out=var, in0=ssq, scalar=1.0 / D, in1=var,
                                                             op0=ALU.mult, op1=ALU.subtract))
        P.wait("vector", m4)
        v4 = P.op("vector", lambda h: h.tensor_scalar(out=var, in0=var, scalar1=EPS, scalar2=None, op0=ALU.add))
        P.wait("gpsimd", v4)
        g1 = P.op("gpsimd", lambda h: h.tensor_tensor(out=var, in0=var, in1=self.halfc[:], op=ALU.pow))
        P.wait("vector", g1)
        v5 = P.op("vector", lambda h: h.reciprocal(out=rstd, in_=var))
        P.wait("vector", v5)
        v6 = P.op("vector", lambda h: h.scalar_tensor_tensor(out=mrs, in0=mu, scalar=-1.0, in1=rstd,
                                                             op0=ALU.mult, op1=ALU.mult))
        P.wait("scalar", v6)
        last = None
        for ob in outs:
            last = P.op("scalar", lambda h, ob=ob: h.activation(out=ob[:], in_=xsb[:], func=AF.Identity,
                                                                bias=mrs, scale=rstd))
        self.prev_apply = last
        return last


def load_weight_bf16(nc, P, stw, src_ap, dst, wstage, wstate, ktiles, cols, scale_ap=None):
    src = src_ap.rearrange("(kt p) c -> kt p c", p=128)
    for kt in range(ktiles):
        m = stw.dma(wstage[:, 0:cols], src[kt], wait=[wstate[0]])
        P.wait("scalar", m)
        wstate[0] = P.op("scalar", lambda h, kt=kt: h.copy(out=dst[:, cols * kt:cols * (kt + 1)],
                                                           in_=wstage[:, 0:cols]))



def emit_p0(nc, P, stk, io, C):
    sb = lambda n, s, d: stk.enter_context(nc.sbuf_tensor(n, s, d))
    identf = C["identf"]; identb = C["identb"]
    x0 = io["x0"]; xcur = io["xcur"]; xlnout = io["xlnout"]
    vsp = io["vsp"]; diagH = io["diagH"]
    ST = C["ST"]
    stA, stB, stC, stD, stE = ST["A"], ST["B"], ST["C"], ST["D"], ST["E"]
    stF, stG, stH, stI = ST["F"], ST["G"], ST["H"], ST["I"]

    wstage = sb("p0_wstage", [128, 1024], F32)
    wq = sb("p0_wq", [128, 8192], BF16)
    wk = sb("p0_wk", [128, 8192], BF16)
    wv = sb("p0_wv", [128, 8192], BF16)
    wo = sb("p0_wo", [128, 8192], BF16)
    wstate = [None]
    for name, dst in (("wq", wq), ("wk", wk), ("wv", wv), ("wo", wo)):
        load_weight_bf16(nc, P, stB, io[name], dst, wstage, wstate, 8, 1024)
    brow = sb("p0_brow", [1, 1024], F32)
    bqd = sb("p0_bqd", [128, 16], F32)
    bkd = sb("p0_bkd", [128, 16], F32)
    bvbc = sb("p0_bvbc", [128, 1024], F32)
    bobc = sb("p0_bobc", [128, 1024], F32)
    wl = [P.mark("scalar")]
    for name, dst in (("bq", bqd), ("bk", bkd)):
        src = io[name].rearrange("(cc x) -> x cc", x=64)
        wl.append(stB.dma(dst[0:64, :], src, allow_slow_non_contiguous=True))
        wl.append(stB.dma(dst[64:128, :], src, allow_slow_non_contiguous=True))
    for name, dst in (("bv", bvbc), ("bo", bobc)):
        wl.append(stB.dma(dst[:], io[name].unsqueeze(0).broadcast_to((128, 1024))))
    P.wait_all("tensor", wl)
    P.wait_all("vector", wl)

    qet = [sb(f"p0_qet{i}", [128, 4096], BF16) for i in range(4)]
    ket = [sb(f"p0_ket{i}", [128, 4096], BF16) for i in range(4)]
    xin = [sb(f"p0_xin{i}", [128, 1024], F32) for i in range(3)]
    xrl = sb("p0_xrl", [128, 1024], F32)
    xlb = sb("p0_xlb", [128, 1024], BF16)
    xlt = [sb(f"p0_xlt{n}", [128, 4096], BF16) for n in range(NB)]
    xltv = [xlt[n].rearrange("p (kt t) -> p kt t", t=512) for n in range(NB)]
    vres = [sb(f"p0_vres{i}", [128, 1024], BF16) for i in range(2)]
    vrl = [sb(f"p0_vrl{i}", [128, 1024], BF16) for i in range(2)]
    jnk1k = sb("p0_jnk1k", [128, 1024], BF16)
    jnkc = sb("p0_jnkc", [128, 2], F32)
    jnk128 = sb("p0_jnk128", [128, 128], F32)
    parts = sb("p0_parts", [128, 4], F32)
    colsum = [sb(f"p0_colsum{n}", [128, 32], F32) for n in range(NB)]
    diagraw = [sb(f"p0_diagraw{n}", [128, 32], F32) for n in range(NB)]
    diagA = [[sb(f"p0_diagA{n}_{h}", [128, 32], F32) for h in range(2)] for n in range(NB)]
    diagTr = sb("p0_diagTr", [32, 128], F32)
    d2t = sb("p0_d2t", [128, 16], F32)
    og = sb("p0_og", [128, 1024], BF16)
    ogt = sb("p0_ogt", [128, 1024], BF16)
    x1f = sb("p0_x1f", [128, 1024], F32)
    xlnf = sb("p0_xlnf", [128, 1024], F32)

    ln = LNHelper(nc, P, stk, "p0_ln", jnk1k)

    PS = C["PS"]

    x0r = x0.rearrange("n (c p) d -> n c p d", p=128)
    xcr = xcur.rearrange("n (c p) d -> n c p d", p=128)
    xlr = xlnout.rearrange("n (c p) d -> n c p d", p=128)
    vspr = vsp.rearrange("n (c p) d -> n c p d", p=128)

    xin_rd = [None] * 3
    xrl_rd = None
    qk_free = [None] * 4
    vres_rd = [None, None]
    vrl_rd = [None, None]
    vspill_m = {}
    og_rd = None
    ogt_rd = None
    x1f_rd = []
    xlnf_rd = None
    d2_rd = None
    diagTr_rd = None
    diagH_m = {}
    diagexp_m = [None, None]
    diagdiv_m = {}
    diagT_done = {}
    colsum_free = [None, None]
    sphase_end = [None, None]
    ppq_ev = [None, None]
    pyy_ev = None
    jnk128_rd = [None]
    jnkc_prev = [None]
    expm = [None, None]

    import os
    NG = int(os.environ.get('P0_NG', '8'))
    for g in range(NG + 1):
        if True:
            P.wait("tensor", sphase_end[0])
            P.wait("tensor", sphase_end[1])

            ppq = [PS[0][:, 512 * i:512 * (i + 1)] for i in range(2)]
            ptt = [PS[0][:, 1024 + 512 * grp:1024 + 512 * (grp + 1)].bitcast(BF16) for grp in range(2)]
            pvv = [PS[1][:, 512 * i:512 * (i + 1)] for i in range(2)]
            pdt = PS[1][0:32, 1536:1664]
            pyy = pvv
            ppq_ev = [None, None]
            pyy_ev = None

            if g >= 1:
                for half in range(2):
                    e = 2 * (g - 1) + half
                    for n in range(NB):
                        P.wait("tensor", diagdiv_m[(n, half)])
                        t1 = P.op("tensor", lambda h, n=n, half=half: h.transpose(
                            pdt[:], diagA[n][half][:, 0:32], identf[:]))
                        diagT_done[(n, half)] = t1
                        P.wait("scalar", t1)
                        P.wait("scalar", diagTr_rd)
                        c1 = P.op("scalar", lambda h: h.copy(out=diagTr[:], in_=pdt[:]))
                        dh_ = diagH[n, e].rearrange("(c p) -> c p", p=128)
                        m = stG.dma(dh_, diagTr[:], wait=[c1])
                        diagH_m[(n, e)] = m
                        diagTr_rd = m
                        P.wait("tensor", c1)

            if g < NG:
                kmark = [None, None]
                xlb_last_cm = None
                for n in range(NB):
                    for li in range(4):
                        lc = 4 * g + li
                        xbuf = xin[li % 3]
                        mx = stA.dma(xbuf[:], x0r[n, lc], wait=[xin_rd[li % 3]])
                        P.wait("vector", mx)
                        P.wait("scalar", mx)
                        P.wait_all("scalar", [xlb_last_cm])
                        lnm = ln.run(xbuf, [xlb])
                        xin_rd[li % 3] = lnm
                        for grp in range(2):
                            pt = ptt[grp]
                            for a in range(4):
                                kt = 4 * grp + a
                                P.wait("tensor", lnm)
                                tm = P.op("tensor", lambda h, kt=kt, a=a, pt=pt: h.transpose(
                                    pt[:, 128 * a:128 * (a + 1)], xlb[:, 128 * kt:128 * (kt + 1)], identb[:]))
                            P.wait("scalar", tm)
                            cm = P.op("scalar", lambda h, grp=grp, n=n, li=li, pt=pt: h.copy(
                                out=xltv[n][:, 4 * grp:4 * grp + 4, 128 * li:128 * (li + 1)],
                                in_=pt[:, 0:512].rearrange("p (a t) -> p a t", t=128)))
                            P.wait("tensor", cm)
                        xlb_last_cm = cm
                    kmark[n] = P.mark("scalar")
                for wsb, bd, dstl in ((wq, bqd, qet), (wk, bkd, ket)):
                    for cc in range(16):
                        pp = ppq[cc % 2]
                        P.wait_all("tensor", kmark + [ppq_ev[cc % 2]])
                        for kt in range(8):
                            for n in range(NB):
                                P.op("tensor", lambda h, n=n, kt=kt, cc=cc, wsb=wsb, pp=pp: h.matmul(
                                    pp[64 * n:64 * (n + 1), :],
                                    lhsT=wsb[:, 1024 * kt + 64 * cc:1024 * kt + 64 * (cc + 1)],
                                    rhs=xltv[n][:, kt, :],
                                    start=(kt == 0), stop=(kt == 7), skip_group_check=True))
                        mmm = P.mark("tensor")
                        P.wait("scalar", mmm)
                        for half in range(2):
                            dst = dstl[(g % 2) * 2 + half]
                            P.wait("scalar", qk_free[(g % 2) * 2 + half])
                            P.op("scalar", lambda h, dst=dst, cc=cc, half=half, bd=bd, pp=pp: h.activation(
                                out=dst[:, cc:4096:16],
                                in_=pp[:, 256 * half:256 * (half + 1)],
                                func=AF.Identity, bias=bd[:, cc:cc + 1], scale=1.0))
                        ppq_ev[cc % 2] = P.mark("scalar")
                for n in range(NB):
                    for li in range(4):
                        lc = 4 * g + li
                        vb = vres[li % 2]
                        P.wait_all("tensor", [kmark[n], vres_rd[li % 2]])
                        for cc in range(2):
                            pvx = pvv[cc]
                            for kt in range(8):
                                P.op("tensor", lambda h, n=n, li=li, kt=kt, cc=cc, pvx=pvx: h.matmul(
                                    pvx[:], lhsT=xltv[n][:, kt, 128 * li:128 * (li + 1)],
                                    rhs=wv[:, 1024 * kt + 512 * cc:1024 * kt + 512 * (cc + 1)],
                                    start=(kt == 0), stop=(kt == 7)))
                        mv = P.mark("tensor")
                        P.wait("vector", mv)
                        P.wait_all("vector", [vres_rd[li % 2]])
                        for cc in range(2):
                            vm = P.op("vector", lambda h, vb=vb, cc=cc: h.tensor_tensor(
                                out=vb[:, 512 * cc:512 * (cc + 1)], in0=pvv[cc][:],
                                in1=bvbc[:, 512 * cc:512 * (cc + 1)], op=ALU.add))
                        P.wait("tensor", vm)
                        sm = stF.dma(vspr[n, lc], vb[:], wait=[vm])
                        vspill_m[(n, lc)] = sm
                        vres_rd[li % 2] = sm

            if g >= 1:
                gg = g - 1
                for n in range(NB):
                    for li in range(4):
                        lc = 4 * gg + li
                        e = lc // 2
                        dsl = diagH[n, e].rearrange("(l p j) -> l p j", p=128, j=16)[lc % 2]
                        md = stD.dma(d2t[:], dsl, wait=[diagH_m[(n, e)], d2_rd])
                        mv = stE.dma(vrl[li % 2][:], vspr[n, lc],
                                     wait=[vspill_m[(n, lc)], vrl_rd[li % 2]])
                        mx = stC.dma(xrl[:], x0r[n, lc], wait=[xrl_rd])
                        P.wait_all("vector", [md, mv, og_rd])
                        for j in range(16):
                            P.op("vector", lambda h, j=j, li=li: h.tensor_scalar(
                                out=og[:, 64 * j:64 * (j + 1)], in0=vrl[li % 2][:, 64 * j:64 * (j + 1)],
                                scalar1=d2t[:, j:j + 1], scalar2=None, op0=ALU.mult))
                        ogm = P.mark("vector")
                        d2_rd = ogm
                        vrl_rd[li % 2] = ogm
                        for grp in range(2):
                            pt = ptt[grp]
                            for a in range(4):
                                kt = 4 * grp + a
                                P.wait("tensor", ogm)
                                tm = P.op("tensor", lambda h, kt=kt, a=a, pt=pt: h.transpose(
                                    pt[:, 128 * a:128 * (a + 1)], og[:, 128 * kt:128 * (kt + 1)], identb[:]))
                            P.wait("scalar", tm)
                            P.wait_all("scalar", [ogt_rd])
                            cm = P.op("scalar", lambda h, grp=grp, pt=pt: h.copy(
                                out=ogt[:, 512 * grp:512 * (grp + 1)], in_=pt[:, 0:512]))
                            P.wait("tensor", cm)
                        og_rd = P.mark("tensor")
                        otm = P.mark("scalar")
                        P.wait_all("tensor", [otm, pyy_ev])
                        for cc in range(2):
                            pyx = pyy[cc]
                            for kt in range(8):
                                P.op("tensor", lambda h, kt=kt, cc=cc, pyx=pyx: h.matmul(
                                    pyx[:], lhsT=ogt[:, 128 * kt:128 * (kt + 1)],
                                    rhs=wo[:, 1024 * kt + 512 * cc:1024 * kt + 512 * (cc + 1)],
                                    start=(kt == 0), stop=(kt == 7)))
                        ogt_rd = P.mark("tensor")
                        mwo = P.mark("tensor")
                        P.wait_all("vector", [mwo, mx] + x1f_rd)
                        for cc in range(2):
                            am = P.op("vector", lambda h, cc=cc: h.tensor_tensor(
                                out=x1f[:, 512 * cc:512 * (cc + 1)], in0=pyy[cc][:],
                                in1=bobc[:, 512 * cc:512 * (cc + 1)], op=ALU.add))
                        P.wait("vector", am)
                        xm = P.op("vector", lambda h: h.tensor_tensor(
                            out=x1f[:], in0=x1f[:], in1=xrl[:], op=ALU.add))
                        pyy_ev = xm
                        xrl_rd = xm
                        mxc = stH.dma(xcr[n, lc], x1f[:], wait=[xm])
                        P.wait("scalar", xm)
                        P.wait_all("scalar", [xlnf_rd])
                        P.wait_all("vector", [xlnf_rd])
                        lnm = ln.run(x1f, [xlnf], dep=xm)
                        m2 = stI.dma(xlr[n, lc], xlnf[:], wait=[lnm])
                        xlnf_rd = m2
                        x1f_rd = [lnm, mxc]

        if g < NG:
            if True:
                P.wait("tensor", P.mark("scalar"))
                P.wait("tensor", P.mark("vector"))
                pA = PS
                for half in range(2):
                    e = 2 * g + half
                    qe = qet[(g % 2) * 2 + half]
                    ke = ket[(g % 2) * 2 + half]
                    for bc in range(32):
                        for ah in range(2):
                            for n in range(NB):
                                P.wait("tensor", expm[n])
                            for ac in range(4):
                                for n in range(NB):
                                    P.op("tensor", lambda h, n=n, bc=bc, ah=ah, ac=ac, qe=qe, ke=ke: h.matmul(
                                        pA[n][:, 512 * ac:512 * (ac + 1)],
                                        lhsT=ke[64 * n:64 * (n + 1), 128 * bc:128 * (bc + 1)],
                                        rhs=qe[64 * n:64 * (n + 1), 2048 * ah + 512 * ac:2048 * ah + 512 * (ac + 1)],
                                        start=True, stop=True))
                            mmdone = P.mark("tensor")
                            if 2048 * ah <= 128 * bc < 2048 * (ah + 1):
                                off = 128 * bc - 2048 * ah
                                for n in range(NB):
                                    P.wait("vector", mmdone)
                                    if bc == 0:
                                        P.wait_all("vector", [diagexp_m[n]])
                                    P.wait("vector", jnk128_rd[0])
                                    dx = P.op("vector", lambda h, n=n, off=off: h.tensor_tensor(
                                        out=jnk128[:], in0=pA[n][:, off:off + 128], in1=identf[:],
                                        op=ALU.mult))
                                    P.wait("vector", dx)
                                    jnk128_rd[0] = P.op("vector", lambda h, n=n, bc=bc: h.tensor_reduce(
                                        out=diagraw[n][:, bc:bc + 1], in_=jnk128[:], axis=AX.X, op=ALU.add))
                            vdone = P.mark("vector")
                            for n in range(NB):
                                P.wait("scalar", mmdone)
                                P.wait("scalar", vdone)
                                if bc == 0 and ah == 0:
                                    P.wait_all("scalar", [colsum_free[n]])
                                P.wait("scalar", jnkc_prev[0])
                                expm[n] = P.op("scalar", lambda h, n=n, ah=ah: h.activation(
                                    out=pA[n][:], in_=pA[n][:], func=AF.Exp, scale=S2,
                                    accum_out=parts[:, 2 * n + ah:2 * n + ah + 1]))
                            P.wait("tensor", P.mark("vector"))
                        for n in range(NB):
                            P.wait("scalar", jnkc_prev[0])
                            P.wait("scalar", expm[n])
                            jnkc_prev[0] = P.op("scalar", lambda h, n=n, bc=bc: h.activation(
                                out=jnkc[:], in_=parts[:, 2 * n:2 * n + 2],
                                func=AF.Identity, accum_out=colsum[n][:, bc:bc + 1]))
                    for n in range(NB):
                        P.wait_all("scalar", [diagT_done.get((n, half))])
                        diagexp_m[n] = P.op("scalar", lambda h, n=n, half=half: h.activation(
                            out=diagA[n][half][:], in_=diagraw[n][:], func=AF.Exp, scale=S2))
                    for n in range(NB):
                        P.wait_all("vector", [diagexp_m[0], diagexp_m[1]])
                        rc = P.op("vector", lambda h, n=n: h.reciprocal(out=colsum[n][:], in_=colsum[n][:]))
                        P.wait("vector", rc)
                        dm = P.op("vector", lambda h, n=n, half=half: h.tensor_tensor(
                            out=diagA[n][half][:], in0=diagA[n][half][:], in1=colsum[n][:], op=ALU.mult))
                        diagdiv_m[(n, half)] = dm
                        colsum_free[n] = dm
                    qk_free[(g % 2) * 2 + half] = P.mark("tensor")
                sphase_end = [P.mark("scalar"), P.mark("vector")]
    P.barrier()


HIDH = 2730
NMT = 22
NROUND = 1

def emit_ffn_mlp(nc, P, stk, io, C, l, xin, xout, extra_res=None, ln_out=None, tag="m"):
    sb = lambda n, s, d: stk.enter_context(nc.sbuf_tensor(n, s, d))
    identb = C["identb"]
    ST = C["ST"]; PS = C["PS"]
    stA, stB, stC, stD = ST["A"], ST["B"], ST["C"], ST["D"]
    stG, stH, stI = ST["G"], ST["H"], ST["I"]

    def fcols(m):
        st_ = 128 * m
        return st_, (128 if m < 21 else HIDH - 128 * 21)

    w1b = sb(f"{tag}_w1b", [128, 8 * 5504], BF16)
    w2b = [sb(f"{tag}_w2b{m}", [128, 1024], BF16) for m in range(NMT)]
    gb_bc = sb(f"{tag}_gbc", [128, 1024], BF16)
    bb_bc = sb(f"{tag}_bbc", [128, 1024], BF16)
    b2bc = sb(f"{tag}_b2bc", [128, 1024], F32)
    b1sb = sb(f"{tag}_b1sb", [128, 2 * NMT], F32)
    xtr = [sb(f"{tag}_xtr0", [128, 1024], F32)]
    wst = xtr[0]
    mlast = [None]

    def W1COL(kt, half, st_):
        return 5504 * kt + 2752 * half + st_

    ldm = []
    ldm.append(stG.dma(gb_bc[:], io["ln_g"][l].unsqueeze(0).broadcast_to((128, 1024))))
    ldm.append(stG.dma(bb_bc[:], io["ln_b"][l].unsqueeze(0).broadcast_to((128, 1024))))
    ldm.append(stB.dma(b2bc[:], io["b2"][l].unsqueeze(0).broadcast_to((128, 1024))))
    zp1 = P.op("gpsimd", lambda h: h.memset(b1sb[32:64, 42:44], 0.0))
    zpad = P.op("gpsimd", lambda h: h.memset(b1sb[64:128, 42:44], 0.0))
    for m in range(NMT):
        st_, wd = fcols(m)
        for half in range(2):
            ldm.append(stB.dma(b1sb[0:wd, 2 * m + half:2 * m + half + 1],
                               io["b1"][l][HIDH * half + st_:HIDH * half + st_ + wd].unsqueeze(1),
                               wait=[zpad], allow_slow_non_contiguous=True))
    for kt in range(8):
        for half in range(2):
            for cchunk in range(3):
                c0 = 1024 * cchunk
                cw = min(1024, HIDH - c0)
                mw = stB.dma(wst[:, 0:cw],
                             io["w1"][l][128 * kt:128 * (kt + 1), HIDH * half + c0:HIDH * half + c0 + cw],
                             wait=[mlast[0]])
                P.wait("scalar", mw)
                mlast[0] = P.op("scalar", lambda h, kt=kt, half=half, c0=c0, cw=cw: h.copy(
                    out=w1b[:, W1COL(kt, half, c0):W1COL(kt, half, c0) + cw], in_=wst[:, 0:cw]))
    for m in range(NMT):
        st_, wd = fcols(m)
        zp = None
        if wd < 128:
            P.op("gpsimd", lambda h, m=m: h.memset(w2b[m][32:64, :], 0.0))
            zp = P.op("gpsimd", lambda h, m=m: h.memset(w2b[m][64:128, :], 0.0))
        mw = stB.dma(wst[0:wd, 0:1024], io["w2"][l][st_:st_ + wd, :], wait=[mlast[0]])
        P.wait("scalar", mw)
        P.wait_all("scalar", [zp])
        mlast[0] = P.op("scalar", lambda h, m=m, wd=wd: h.copy(out=w2b[m][0:wd, :], in_=wst[0:wd, 0:1024]))
    wload = [P.mark("scalar"), P.mark("gpsimd")] + ldm
    P.wait_all("tensor", wload)
    P.wait_all("vector", wload)
    P.wait_all("scalar", ldm)

    ln = LNHelper(nc, P, stk, f"{tag}_ln", sb(f"{tag}_lnjnk", [128, 1024], BF16))
    xlb = sb(f"{tag}_xlb", [128, 1024], BF16)
    xlt = sb(f"{tag}_xlt", [128, 4096], BF16)
    xltv = xlt.rearrange("p (kt t) -> p kt t", t=512)
    hsh = [sb(f"{tag}_hsh{i}", [128, 1024], F32) for i in range(2)]
    xmc = [sb(f"{tag}_xmc0", [128, 1024], F32)]
    ggt = [sb(f"{tag}_ggt0", [128, 512], BF16)]
    NHP = NMT // NROUND
    hpt = [sb(f"{tag}_hpt{m}", [128, 512], BF16) for m in range(NHP)]
    hz1 = P.op("gpsimd", lambda h: h.memset(hpt[NHP - 1][32:64, :], 0.0))
    hz2 = P.op("gpsimd", lambda h: h.memset(hpt[NHP - 1][64:128, :], 0.0))
    P.wait_all("tensor", [hz1, hz2])
    P.wait_all("vector", [hz1, hz2])
    xof = sb(f"{tag}_xof", [128, 1024], F32)
    xlnf = sb(f"{tag}_xlnf", [128, 1024], F32)

    xinr = xin.rearrange("n (c p) d -> n c p d", p=128)
    xinf = xin.rearrange("n t d -> n (t d)")
    xoutr = xout.rearrange("n (c p) d -> n c p d", p=128)
    xtrr = extra_res.rearrange("n (c p) d -> n c p d", p=128) if extra_res is not None else None
    lnoutr = ln_out.rearrange("n (c p) d -> n c p d", p=128) if ln_out is not None else None

    pw1 = [PS[0][:, 0:512], PS[0][:, 512:1024]]
    ptt = [PS[0][:, 1024 + 512 * i:1536 + 512 * i].bitcast(BF16) for i in range(2)]
    pw2 = [PS[1][:, 512 * cc:512 * (cc + 1)] for cc in range(2)]

    hsh_rd = [None, None]
    xmc_rd = [None]
    xtr_rd = [None]
    xlb_rd = None
    hpt_rd = [None] * NHP
    ggt_rd = [None]
    w1ev = [None, None]
    w2ev = [None, None]
    xof_rd = []
    xlnf_rd = None
    xlt_rd = None

    for n in range(NB):
        for g in range(8):
            for li in range(4):
                lc = 8 * g + li if False else 4 * g + li
                lc = 4 * g + li
                hb = hsh[li % 2]
                m1 = stA.dma(hb[:, 0:512], xinr[n, lc][:, 0:512], wait=[hsh_rd[li % 2]])
                if lc == 0:
                    zm = P.op("gpsimd", lambda h, hb=hb: h.memset(hb[0:1, 512:1024], 0.0))
                    m2 = stA.dma(hb[1:128, 512:1024], xin[n][0:127, 512:1024], wait=[zm])
                else:
                    m2 = stA.dma(hb[:, 512:1024], xin[n][128 * lc - 1:128 * lc + 127, 512:1024])
                P.wait_all("vector", [m1, m2])
                P.wait_all("scalar", [m1, m2])
                lnm = ln.run(hb, [hb], dep=None)
                P.wait("vector", lnm)
                v1 = P.op("vector", lambda h, hb=hb: h.tensor_tensor(out=hb[:], in0=hb[:], in1=gb_bc[:], op=ALU.mult))
                P.wait("vector", v1)
                P.wait_all("vector", [xlb_rd])
                v2 = P.op("vector", lambda h, hb=hb: h.tensor_tensor(out=xlb[:], in0=hb[:], in1=bb_bc[:], op=ALU.add))
                hsh_rd[li % 2] = v2
                for grp in range(2):
                    pt = ptt[grp]
                    P.wait_all("tensor", [v2, w1ev[0], w1ev[1]])
                    for a in range(4):
                        kt = 4 * grp + a
                        tm = P.op("tensor", lambda h, kt=kt, a=a, pt=pt: h.transpose(
                            pt[:, 128 * a:128 * (a + 1)], xlb[:, 128 * kt:128 * (kt + 1)], identb[:]))
                    P.wait("scalar", tm)
                    P.wait_all("scalar", [xlt_rd])
                    cm = P.op("scalar", lambda h, grp=grp, li=li, pt=pt: h.copy(
                        out=xltv[:, 4 * grp:4 * grp + 4, 128 * li:128 * (li + 1)],
                        in_=pt[:, 0:512].rearrange("p (a t) -> p a t", t=128)))
                    P.wait("tensor", cm)
                xlb_rd = cm
            tdone = P.mark("scalar")
            for rnd in range(NROUND):
                for mi in range(NHP):
                    m = rnd * NHP + mi
                    st_, wd = fcols(m)
                    for half in (1, 0):
                        pb = pw1[half]
                        P.wait_all("tensor", [tdone, w1ev[half]])
                        for kt in range(8):
                            P.op("tensor", lambda h, kt=kt, m=m, half=half, st_=st_, wd=wd, pb=pb: h.matmul(
                                pb[0:wd, :], lhsT=w1b[:, W1COL(kt, half, st_):W1COL(kt, half, st_) + wd],
                                rhs=xltv[:, kt, :], start=(kt == 0), stop=(kt == 7)))
                    mmw = P.mark("tensor")
                    P.wait("scalar", mmw)
                    P.wait_all("scalar", [ggt_rd[0]])
                    gm = P.op("scalar", lambda h, m=m, wd=wd: h.activation(
                        out=ggt[0][0:wd, :], in_=pw1[1][0:wd, :], func=AF.Gelu,
                        bias=b1sb[0:wd, 2 * m + 1:2 * m + 2], scale=1.0))
                    w1ev[1] = gm
                    P.wait_all("vector", [mmw, gm, hpt_rd[mi]])
                    am = P.op("vector", lambda h, m=m, wd=wd, mi=mi: h.scalar_tensor_tensor(
                        out=hpt[mi][0:wd, :], in0=pw1[0][0:wd, :], scalar=b1sb[0:wd, 2 * m:2 * m + 1],
                        in1=ggt[0][0:wd, :], op0=ALU.add, op1=ALU.mult))
                    w1ev[0] = am
                    ggt_rd[0] = am
                    hpt_rd[mi] = am
                for tc in range(4):
                    lc = 4 * g + tc
                    for cc in range(2):
                        P.wait_all("tensor", [hpt_rd[NHP - 1], w2ev[cc] if rnd == 0 else None])
                        for mi in range(NHP):
                            m = rnd * NHP + mi
                            P.wait("tensor", hpt_rd[mi])
                            P.op("tensor", lambda h, mi=mi, m=m, tc=tc, cc=cc, rnd=rnd: h.matmul(
                                pw2[cc][:], lhsT=hpt[mi][:, 128 * tc:128 * (tc + 1)],
                                rhs=w2b[m][:, 512 * cc:512 * (cc + 1)],
                                start=(rnd == 0 and mi == 0), stop=(rnd == NROUND - 1 and mi == NHP - 1)))
                    if rnd < NROUND - 1:
                        continue
                    mmf = P.mark("tensor")
                    mx = stC.dma(xmc[0][:], xinr[n, lc], wait=[xmc_rd[0]])
                    wl = [mmf, mx]
                    if extra_res is not None:
                        mt = stD.dma(xtr[0][:], xtrr[n, lc], wait=[xtr_rd[0]])
                        wl.append(mt)
                    P.wait_all("vector", wl + xof_rd)
                    for cc in range(2):
                        a1 = P.op("vector", lambda h, cc=cc: h.tensor_tensor(
                            out=xof[:, 512 * cc:512 * (cc + 1)], in0=pw2[cc][:],
                            in1=b2bc[:, 512 * cc:512 * (cc + 1)], op=ALU.add))
                    P.wait("vector", a1)
                    a2 = P.op("vector", lambda h: h.tensor_tensor(
                        out=xof[:], in0=xof[:], in1=xmc[0][:], op=ALU.add))
                    xmc_rd[0] = a2
                    lastv = a2
                    if extra_res is not None:
                        P.wait("vector", a2)
                        a3 = P.op("vector", lambda h: h.tensor_tensor(
                            out=xof[:], in0=xof[:], in1=xtr[0][:], op=ALU.add))
                        xtr_rd[0] = a3
                        lastv = a3
                    P.wait("tensor", lastv)
                    w2ev[0] = lastv
                    w2ev[1] = lastv
                    mo = stH.dma(xoutr[n, lc], xof[:], wait=[lastv])
                    xof_rd = [mo]
                    if ln_out is not None:
                        P.wait_all("scalar", [xlnf_rd])
                        lnm2 = ln.run(xof, [xlnf], dep=lastv)
                        xof_rd.append(lnm2)
                        m2o = stI.dma(lnoutr[n, lc], xlnf[:], wait=[lnm2])
                        xlnf_rd = m2o
            xlt_rd = P.mark("tensor")
    P.barrier()


def emit_ffn_attn(nc, P, stk, io, C, l, xin, xout, tag="a"):
    sb = lambda n, s, d: stk.enter_context(nc.sbuf_tensor(n, s, d))
    identb = C["identb"]
    ST = C["ST"]; PS = C["PS"]
    stA, stB, stD = ST["A"], ST["B"], ST["D"]
    stH = ST["H"]

    wqkvb = sb(f"{tag}_wqkv", [128, 8 * 1536], BF16)
    wob = sb(f"{tag}_wob", [128, 4096], BF16)
    wst = sb(f"{tag}_wst", [128, 1536], F32)
    mlast = [None]
    src = io["wqkv"][l].rearrange("(kt p) c -> kt p c", p=128)
    for kt in range(8):
        mw = stB.dma(wst[:], src[kt], wait=[mlast[0]])
        P.wait("scalar", mw)
        mlast[0] = P.op("scalar", lambda h, kt=kt: h.copy(
            out=wqkvb[:, 1536 * kt:1536 * (kt + 1)], in_=wst[:]))
    wosrc = io["wo"][l].rearrange("(hh p) c -> hh p c", p=128)
    for hh in range(4):
        mw = stB.dma(wst[:, 0:1024], wosrc[hh], wait=[mlast[0]])
        P.wait("scalar", mw)
        mlast[0] = P.op("scalar", lambda h, hh=hh: h.copy(
            out=wob[:, 1024 * hh:1024 * (hh + 1)], in_=wst[:, 0:1024]))
    qsb = sb(f"{tag}_qsb", [128, 64], F32)
    qsw = sb(f"{tag}_qsw", [128, 64], F32)
    ksb = sb(f"{tag}_ksb", [128, 64], F32)
    ksw = sb(f"{tag}_ksw", [128, 64], F32)
    wl2 = [stB.dma(qsb[:], io["q_scale"][l].unsqueeze(0).broadcast_to((128, 64))),
           stB.dma(qsw[:, 0:32], io["q_scale"][l][32:64].unsqueeze(0).broadcast_to((128, 32))),
           stB.dma(qsw[:, 32:64], io["q_scale"][l][0:32].unsqueeze(0).broadcast_to((128, 32))),
           stB.dma(ksb[:], io["k_scale"][l].unsqueeze(0).broadcast_to((128, 64))),
           stB.dma(ksw[:, 0:32], io["k_scale"][l][32:64].unsqueeze(0).broadcast_to((128, 32))),
           stB.dma(ksw[:, 32:64], io["k_scale"][l][0:32].unsqueeze(0).broadcast_to((128, 32)))]
    halfc16 = sb(f"{tag}_halfc16", [128, 16], F32)
    hm = P.op("gpsimd", lambda h: h.memset(halfc16[:], 0.5))
    wload = [P.mark("scalar"), hm] + wl2
    P.wait_all("tensor", wload)
    P.wait_all("vector", wload)
    P.wait_all("gpsimd", wload)

    mfirst = C["mfirst"]; mlastm = C["mlast"]
    rotc = io["c_rot"].rearrange("(c p) x -> c p x", p=128)

    hb = sb(f"{tag}_hb", [128, 1024], BF16)
    hTs = sb(f"{tag}_hTs", [128, 1024], BF16)
    xsh = [sb(f"{tag}_xsh{i}", [128, 512], F32) for i in range(2)]
    xmc = [sb(f"{tag}_xmc{i}", [128, 1024], F32) for i in range(8)]
    rch = sb(f"{tag}_rch", [128, 256], F32)
    rote = sb(f"{tag}_rote", [128, 256], F32)
    qf = sb(f"{tag}_qf", [128, 512], F32)
    kf = sb(f"{tag}_kf", [128, 512], F32)
    qsq = sb(f"{tag}_qsq", [128, 512], F32)
    ssn = sb(f"{tag}_ssn", [128, 32], F32)
    tmp1 = sb(f"{tag}_tmp1", [128, 256], F32)
    tmp2 = sb(f"{tag}_tmp2", [128, 256], F32)
    qr = sb(f"{tag}_qr", [128, 512], BF16)
    kr = sb(f"{tag}_kr", [128, 512], BF16)
    qT = sb(f"{tag}_qT", [128, 2048], BF16)
    kT = sb(f"{tag}_kT", [128, 4096], BF16)
    vbr = sb(f"{tag}_vbr", [128, 4096], BF16)
    absb = [sb(f"{tag}_absb{i}", [128, 640], BF16) for i in range(2)]
    atb = [sb(f"{tag}_atb{i}", [128, 640], BF16) for i in range(2)]
    obT = [sb(f"{tag}_obT{hh}", [128, 512], BF16) for hh in range(4)]
    nmax = sb(f"{tag}_nmax", [128, 2], F32)
    zrow = sb(f"{tag}_zrow", [128, 4], F32)
    xof = sb(f"{tag}_xof", [128, 1024], F32)
    zm0 = P.op("gpsimd", lambda h: h.memset(zrow[:], 0.0))
    P.wait("scalar", zm0)

    xinr = xin.rearrange("n (c p) d -> n c p d", p=128)
    xoutr = xout.rearrange("n (c p) d -> n c p d", p=128)

    pqkv = [PS[0][:, 0:512], PS[0][:, 512:1024]]
    ptt = [PS[0][:, 1024 + 256 * i:1280 + 256 * i].bitcast(BF16) for i in range(2)]
    pwo = PS[0][:, 1536:2048]
    pS = [PS[1][:, 0:640], PS[1][:, 640:1280]]
    pat = PS[1][:, 1280:1792].bitcast(BF16)
    pO = [PS[1][0:128, 1792 + 128 * i:1920 + 128 * i] for i in range(2)]

    hb_rd = None
    hTs_rd = None
    xsh_rd = [None, None]
    xmc_rd = [None] * 8
    qkv_ev = [None, None]
    vwr = {}
    qf_rd = None; kf_rd = None
    qr_rd = None; kr_rd = None
    ring_rd = {}
    absb_rd = [None, None]
    atb_rd = [None, None]
    obT_free = None
    pS_rd = [None, None]
    pO_rd = [None, None]
    rch_rd = None; rote_rd = None
    xof_rd = []
    tmp_rd = None
    ssn_rd = None
    qsq_rd = None
    pwo_rd = None
    pat_rd = None

    for n in range(NB):
        for w in range(8):
            for ci in range(4):
                lc = 4 * w + ci
                slot = (w % 2) * 4 + ci
                xslot = lc % 8
                mx = stA.dma(xmc[xslot][:], xinr[n, lc], wait=[xmc_rd[xslot]])
                sh = xsh[ci % 2]
                if lc == 0:
                    zm = P.op("gpsimd", lambda h, sh=sh: h.memset(sh[0:1, :], 0.0))
                    m2 = stA.dma(sh[1:128, :], xin[n][0:127, 512:1024], wait=[xsh_rd[ci % 2], zm])
                else:
                    m2 = stA.dma(sh[:], xin[n][128 * lc - 1:128 * lc + 127, 512:1024],
                                 wait=[xsh_rd[ci % 2]])
                P.wait_all("scalar", [mx, m2, hb_rd])
                c1 = P.op("scalar", lambda h, xslot=xslot: h.copy(out=hb[:, 0:512], in_=xmc[xslot][:, 0:512]))
                c2 = P.op("scalar", lambda h, sh=sh: h.copy(out=hb[:, 512:1024], in_=sh[:]))
                xsh_rd[ci % 2] = c2
                for grp in range(2):
                    pt = ptt[grp]
                    P.wait_all("tensor", [c2, qkv_ev[0], qkv_ev[1]])
                    for a in range(4):
                        kt = 4 * grp + a
                        tm = P.op("tensor", lambda h, kt=kt, a=a, pt=pt: h.transpose(
                            pt[:, 128 * a:128 * (a + 1)], hb[:, 128 * kt:128 * (kt + 1)], identb[:]))
                    P.wait("scalar", tm)
                    P.wait_all("scalar", [hTs_rd])
                    cm = P.op("scalar", lambda h, grp=grp, pt=pt: h.copy(
                        out=hTs.rearrange("p (kt t) -> p kt t", t=128)[:, 4 * grp:4 * grp + 4, :],
                        in_=pt[:, 0:512].rearrange("p (a t) -> p a t", t=128)))
                    P.wait("tensor", cm)
                hb_rd = cm
                hTdone = P.mark("scalar")
                hTv = hTs.rearrange("p (kt t) -> p kt t", t=128)
                for cp in range(3):
                    pb = pqkv[cp % 2]
                    P.wait_all("tensor", [hTdone, qkv_ev[cp % 2]])
                    for kt in range(8):
                        P.op("tensor", lambda h, kt=kt, cp=cp, pb=pb: h.matmul(
                            pb[:], lhsT=hTv[:, kt, :],
                            rhs=wqkvb[:, 1536 * kt + 512 * cp:1536 * kt + 512 * (cp + 1)],
                            start=(kt == 0), stop=(kt == 7)))
                    mmq = P.mark("tensor")
                    if cp == 0:
                        P.wait_all("vector", [mmq, qf_rd])
                        qkv_ev[0] = P.op("vector", lambda h, pb=pb: h.tensor_copy(out=qf[:], in_=pb[:]))
                    elif cp == 1:
                        P.wait_all("vector", [mmq, kf_rd])
                        qkv_ev[1] = P.op("vector", lambda h, pb=pb: h.tensor_copy(out=kf[:], in_=pb[:]))
                    else:
                        P.wait_all("scalar", [mmq, ring_rd.get(("v", slot))])
                        qkv_ev[0] = P.op("scalar", lambda h, pb=pb, slot=slot: h.copy(
                            out=vbr[:, 512 * slot:512 * (slot + 1)], in_=pb[:]))
                        vwr[slot] = qkv_ev[0]
                mr = stD.dma(rch[:], rotc[lc], wait=[rch_rd])
                P.wait_all("vector", [mr, rote_rd])
                r1 = P.op("vector", lambda h: h.tensor_tensor(out=rote[:, 0:64], in0=rch[:, 0:64], in1=qsb[:], op=ALU.mult))
                r2 = P.op("vector", lambda h: h.tensor_tensor(out=rote[:, 64:128], in0=rch[:, 64:128], in1=qsw[:], op=ALU.mult))
                r3 = P.op("vector", lambda h: h.tensor_tensor(out=rote[:, 128:192], in0=rch[:, 128:192], in1=ksb[:], op=ALU.mult))
                r4 = P.op("vector", lambda h: h.tensor_tensor(out=rote[:, 192:256], in0=rch[:, 192:256], in1=ksw[:], op=ALU.mult))
                rch_rd = r4
                for which, (src_, dst0) in enumerate(((qf, 0), (kf, 8))):
                    P.wait_all("scalar", [qkv_ev[which], qsq_rd])
                    sq = P.op("scalar", lambda h, src_=src_: h.activation(out=qsq[:], in_=src_[:], func=AF.Square))
                    P.wait_all("vector", [sq, ssn_rd])
                    qsq_rd = P.op("vector", lambda h, dst0=dst0: h.tensor_reduce(
                        out=ssn[:, dst0:dst0 + 8], in_=qsq[:].rearrange("p (hd x) -> p hd x", x=64),
                        axis=AX.X, op=ALU.add))
                P.wait_all("gpsimd", [qsq_rd])
                gp1 = P.op("gpsimd", lambda h: h.tensor_tensor(out=ssn[:, 16:32], in0=ssn[:, 0:16],
                                                               in1=halfc16[:], op=ALU.pow))
                P.wait("vector", gp1)
                v1 = P.op("vector", lambda h: h.tensor_scalar(out=ssn[:, 16:32], in0=ssn[:, 16:32],
                                                              scalar1=1e-12, scalar2=None, op0=ALU.max))
                P.wait("vector", v1)
                rn_m = P.op("vector", lambda h: h.reciprocal(out=ssn[:, 16:32], in_=ssn[:, 16:32]))
                ssn_rd = rn_m
                qdone = kdone = None
                for which, (src_, rno, tab0, dstt) in enumerate(((qf, 16, 0, qr), (kf, 24, 128, kr))):
                    s3 = src_[:].rearrange("p (hd x) -> p hd x", x=64)
                    P.wait_all("vector", [rn_m, qkv_ev[which]])
                    u1 = P.op("vector", lambda h, s3=s3, rno=rno: h.tensor_tensor(
                        out=s3, in0=s3, in1=ssn[:, rno:rno + 8].unsqueeze(2).broadcast_to((128, 8, 64)),
                        op=ALU.mult))
                    P.wait_all("vector", [u1, r4, tmp_rd])
                    d3 = dstt[:].rearrange("p (hd x) -> p hd x", x=64)
                    t13 = tmp1[:].rearrange("p (hd x) -> p hd x", x=32)
                    t23 = tmp2[:].rearrange("p (hd x) -> p hd x", x=32)
                    ta = P.op("vector", lambda h, s3=s3, t13=t13, tab0=tab0: h.tensor_tensor(
                        out=t13, in0=s3[:, :, 0:32],
                        in1=rote[:, tab0:tab0 + 32].unsqueeze(1).broadcast_to((128, 8, 32)), op=ALU.mult))
                    tb = P.op("vector", lambda h, s3=s3, t23=t23, tab0=tab0: h.tensor_tensor(
                        out=t23, in0=s3[:, :, 32:64],
                        in1=rote[:, tab0 + 64:tab0 + 96].unsqueeze(1).broadcast_to((128, 8, 32)), op=ALU.mult))
                    P.wait_all("vector", [ta, tb, qr_rd if which == 0 else kr_rd])
                    tc_ = P.op("vector", lambda h, d3=d3, t13=t13, t23=t23: h.tensor_tensor(
                        out=d3[:, :, 0:32], in0=t13, in1=t23, op=ALU.subtract))
                    P.wait("vector", tc_)
                    ta2 = P.op("vector", lambda h, s3=s3, t13=t13, tab0=tab0: h.tensor_tensor(
                        out=t13, in0=s3[:, :, 32:64],
                        in1=rote[:, tab0 + 32:tab0 + 64].unsqueeze(1).broadcast_to((128, 8, 32)), op=ALU.mult))
                    tb2 = P.op("vector", lambda h, s3=s3, t23=t23, tab0=tab0: h.tensor_tensor(
                        out=t23, in0=s3[:, :, 0:32],
                        in1=rote[:, tab0 + 96:tab0 + 128].unsqueeze(1).broadcast_to((128, 8, 32)), op=ALU.mult))
                    P.wait_all("vector", [ta2, tb2])
                    td = P.op("vector", lambda h, d3=d3, t13=t13, t23=t23: h.tensor_tensor(
                        out=d3[:, :, 32:64], in0=t13, in1=t23, op=ALU.add))
                    tmp_rd = td
                    if which == 0:
                        qdone = td
                    else:
                        kdone = td
                rote_rd = tmp_rd
                for which, (srcr, done) in enumerate(((qr, qdone), (kr, kdone))):
                    pt = ptt[which]
                    P.wait_all("tensor", [done, hb_rd])
                    for hh in range(4):
                        tm = P.op("tensor", lambda h, hh=hh, srcr=srcr, pt=pt: h.transpose(
                            pt[:, 128 * hh:128 * (hh + 1)], srcr[:, 128 * hh:128 * (hh + 1)], identb[:]))
                    P.wait("scalar", tm)
                    if which == 0:
                        P.wait_all("scalar", [ring_rd.get(("q", ci))])
                        ev = P.op("scalar", lambda h, ci=ci, pt=pt: h.copy(
                            out=qT.rearrange("p (hh t) -> p hh t", t=512)[:, :, 128 * ci:128 * (ci + 1)],
                            in_=pt[:, 0:512].rearrange("p (a t) -> p a t", t=128)))
                        qr_rd = ev
                    else:
                        P.wait_all("scalar", [ring_rd.get(("k", slot))])
                        ev = P.op("scalar", lambda h, w=w, ci=ci, pt=pt: h.copy(
                            out=kT.rearrange("p (hh wp t) -> p hh wp t", wp=2, t=512)
                                [:, :, w % 2, 128 * ci:128 * (ci + 1)],
                            in_=pt[:, 0:512].rearrange("p (a t) -> p a t", t=128)))
                        kr_rd = ev
                    P.wait("tensor", ev)
            qkvdone = [P.mark("scalar"), P.mark("vector")]
            kTv = kT.rearrange("p (hh wp t) -> p hh wp t", wp=2, t=512)
            qTv = qT.rearrange("p (hh t) -> p hh t", t=512)
            for hh in range(4):
                for qc in range(4):
                    j0 = (4 - qc) if w == 0 else 0
                    nj = 5 - j0
                    P.wait_all("tensor", qkvdone + [pS_rd[0], pS_rd[1]])
                    for j in range(j0, 5):
                        kpb = 512 * (w - 1) + 128 * (qc + j)
                        wslot = (kpb // 512) % 2
                        koff = kpb % 512
                        for half in range(2):
                            P.op("tensor", lambda h, hh=hh, half=half, qc=qc, j=j,
                                 wslot=wslot, koff=koff: h.matmul(
                                pS[half][:, 128 * j:128 * (j + 1)],
                                lhsT=qTv[64 * half:64 * (half + 1), hh, 128 * qc:128 * (qc + 1)],
                                rhs=kTv[64 * half:64 * (half + 1), hh, wslot, koff:koff + 128],
                                start=True, stop=True, skip_group_check=True))
                    mS = P.mark("tensor")
                    P.wait_all("vector", [mS])
                    for half in range(2):
                        if j0 == 0:
                            P.op("vector", lambda h, half=half: h.tensor_tensor(
                                out=pS[half][:, 0:128], in0=pS[half][:, 0:128], in1=mfirst[:], op=ALU.add))
                        vl = P.op("vector", lambda h, half=half: h.tensor_tensor(
                            out=pS[half][:, 512:640], in0=pS[half][:, 512:640], in1=mlastm[:], op=ALU.add))
                        P.wait("vector", vl)
                        rm = P.op("vector", lambda h, half=half, j0=j0: h.tensor_reduce(
                            out=nmax[:, half:half + 1], in_=pS[half][:, 128 * j0:640], axis=AX.X,
                            op=ALU.max, negate=True))
                        P.wait_all("scalar", [rm, absb_rd[half]])
                        ex = P.op("scalar", lambda h, half=half, j0=j0: h.activation(
                            out=absb[half][:, 128 * j0:640], in_=pS[half][:, 128 * j0:640], func=AF.Exp,
                            bias=nmax[:, half:half + 1], accum_out=zrow[:, half:half + 1]))
                        exm = P.op("scalar", lambda h, half=half: h.copy(out=zrow[:, 2 + half:3 + half],
                                                                         in_=zrow[:, 2 + half:3 + half]))
                        pS_rd[half] = ex
                        P.wait_all("vector", [exm])
                        rz = P.op("vector", lambda h, half=half: h.reciprocal(
                            out=zrow[:, half:half + 1], in_=zrow[:, half:half + 1]))
                        P.wait_all("gpsimd", [rz, ex, atb_rd[half]])
                        gn = P.op("gpsimd", lambda h, half=half, j0=j0: h.tensor_scalar(
                            out=absb[half][:, 128 * j0:640], in0=absb[half][:, 128 * j0:640],
                            scalar1=zrow[:, half:half + 1], scalar2=None, op0=ALU.mult))
                        absb_rd[half] = gn
                        P.wait_all("tensor", [gn, pat_rd])
                        for j in range(j0, 5):
                            tm = P.op("tensor", lambda h, j=j, j0=j0, half=half: h.transpose(
                                pat[:, 128 * (j - j0):128 * (j - j0 + 1)],
                                absb[half][:, 128 * j:128 * (j + 1)], identb[:]))
                        P.wait("scalar", tm)
                        P.wait_all("scalar", [atb_rd[half]])
                        aev = P.op("scalar", lambda h, half=half, nj=nj: h.copy(
                            out=atb[half][:, 0:128 * nj], in_=pat[:, 0:128 * nj]))
                        pat_rd = aev
                        atb_rd[half] = aev
                    atdone = P.mark("scalar")
                    opar = qc % 2
                    P.wait_all("tensor", [atdone, pO_rd[opar]])
                    for ji, j in enumerate(range(j0, 5)):
                        kpb = 512 * (w - 1) + 128 * (qc + j)
                        slot2 = ((kpb // 512) % 2) * 4 + (kpb % 512) // 128
                        for half in range(2):
                            hgl = 2 * hh + half
                            P.op("tensor", lambda h, ji=ji, half=half, slot2=slot2, hgl=hgl,
                                 nj=nj, opar=opar: h.matmul(
                                pO[opar][64 * half:64 * (half + 1), :],
                                lhsT=vbr[:, 512 * slot2 + 64 * hgl:512 * slot2 + 64 * hgl + 64],
                                rhs=atb[half][:, 128 * ji:128 * (ji + 1)],
                                start=(ji == 0), stop=(ji == nj - 1), skip_group_check=True))
                    mO = P.mark("tensor")
                    atb_rd[0] = mO
                    atb_rd[1] = mO
                    P.wait_all("scalar", [mO, obT_free])
                    oev = P.op("scalar", lambda h, hh=hh, qc=qc, opar=opar: h.copy(
                        out=obT[hh][:, 128 * qc:128 * (qc + 1)], in_=pO[opar][:]))
                    pO_rd[opar] = oev
            obdone = P.mark("scalar")
            attn_t = P.mark("tensor")
            for ci in range(4):
                ring_rd[("k", ((w - 1) % 2) * 4 + ci)] = attn_t
                ring_rd[("v", ((w - 1) % 2) * 4 + ci)] = attn_t
                ring_rd[("q", ci)] = attn_t
            av = None
            for tc in range(4):
                lc = 4 * w + tc
                xslot = lc % 8
                for cc in range(2):
                    P.wait_all("tensor", [obdone, pwo_rd])
                    for hh in range(4):
                        P.op("tensor", lambda h, hh=hh, tc=tc, cc=cc: h.matmul(
                            pwo[:], lhsT=obT[hh][:, 128 * tc:128 * (tc + 1)],
                            rhs=wob[:, 1024 * hh + 512 * cc:1024 * hh + 512 * (cc + 1)],
                            start=(hh == 0), stop=(hh == 3)))
                    mw = P.mark("tensor")
                    P.wait_all("vector", [mw] + xof_rd)
                    av = P.op("vector", lambda h, cc=cc, xslot=xslot: h.tensor_tensor(
                        out=xof[:, 512 * cc:512 * (cc + 1)], in0=pwo[:],
                        in1=xmc[xslot][:, 512 * cc:512 * (cc + 1)], op=ALU.add))
                    pwo_rd = av
                xmc_rd[xslot] = av
                mo = stH.dma(xoutr[n, lc], xof[:], wait=[av])
                xof_rd = [mo]
            obT_free = P.mark("tensor")
    P.barrier()


def emit_attn_b(nc, P, stk, io, C, xin, xcur, xout, ln_out, tag="b"):
    sb = lambda n, s, d: stk.enter_context(nc.sbuf_tensor(n, s, d))
    identf = C["identf"]
    ST = C["ST"]; PS = C["PS"]
    stA, stB, stC, stD = ST["A"], ST["B"], ST["C"], ST["D"]
    stH, stI = ST["H"], ST["I"]

    watb = sb(f"{tag}_watb", [128, 8192], BF16)
    wbtb = sb(f"{tag}_wbtb", [128, 8192], BF16)
    cwtb = sb(f"{tag}_cwtb", [128, 8192], BF16)
    wst = sb(f"{tag}_wst", [128, 1024], F32)
    wstb = sb(f"{tag}_wstb", [128, 1024], BF16)
    ptr = PS[0][:, 0:512]
    ptb = PS[0][:, 0:512].bitcast(BF16)
    mlast = [None]
    trd = [None]
    for name, dst in (("W_alpha_w", watb), ("W_B_w", wbtb), ("C_w", cwtb)):
        srcw = io[name].rearrange("(jt p) d -> jt p d", p=128)
        for jt in range(8):
            mw = stB.dma(wst[:], srcw[jt], wait=[mlast[0]])
            P.wait("scalar", mw)
            cb = P.op("scalar", lambda h: h.copy(out=wstb[:], in_=wst[:]))
            mlast[0] = cb
            for kt in range(8):
                P.wait_all("tensor", [cb, trd[0]])
                t1 = P.op("tensor", lambda h, kt=kt: h.transpose(
                    ptb[:, 0:128], wstb[:, 128 * kt:128 * (kt + 1)], C["identb"][:]))
                P.wait("scalar", t1)
                trd[0] = P.op("scalar", lambda h, kt=kt, jt=jt, dst=dst: h.copy(
                    out=dst[:, 1024 * kt + 128 * jt:1024 * kt + 128 * jt + 128],
                    in_=ptb[:, 0:128]))
    bacol = sb(f"{tag}_bacol", [128, 8], F32)
    bbcol = sb(f"{tag}_bbcol", [128, 8], F32)
    dbc = sb(f"{tag}_dbc", [128, 1024], F32)
    wl = [stB.dma(bacol[:], io["W_alpha_b"].rearrange("(k p) -> p k", p=128), allow_slow_non_contiguous=True),
          stB.dma(bbcol[:], io["W_B_b"].rearrange("(k p) -> p k", p=128), allow_slow_non_contiguous=True),
          stB.dma(dbc[:], io["Dv"].unsqueeze(0).broadcast_to((128, 1024))),
          P.mark("scalar")]
    P.wait_all("tensor", wl)
    P.wait_all("vector", wl)

    ln = LNHelper(nc, P, stk, f"{tag}_ln", sb(f"{tag}_lnjnk", [128, 1024], BF16))
    uc = [sb(f"{tag}_uc{i}", [128, 1024], F32) for i in range(4)]
    ub = sb(f"{tag}_ub", [128, 1024], BF16)
    uT = sb(f"{tag}_uT", [128, 4096], BF16)
    uTv = uT.rearrange("p (kt t) -> p kt t", t=512)
    alf = sb(f"{tag}_alf", [128, 512], F32)
    buf_ = sb(f"{tag}_buf", [128, 512], F32)
    hsf = sb(f"{tag}_hsf", [128, 512], F32)
    hsb = sb(f"{tag}_hsb", [128, 4096], BF16)
    hsbv = hsb.rearrange("p (nc t) -> p nc t", t=512)
    carry = sb(f"{tag}_carry", [128, 8], F32)
    xcc = [sb(f"{tag}_xcc{i}", [128, 1024], F32) for i in range(2)]
    xof = sb(f"{tag}_xof", [128, 1024], F32)
    xlnf = sb(f"{tag}_xlnf", [128, 1024], F32)

    xinr = xin.rearrange("n (c p) d -> n c p d", p=128)
    xcurr = xcur.rearrange("n (c p) d -> n c p d", p=128)
    xoutr = xout.rearrange("n (c p) d -> n c p d", p=128)
    lnoutr = ln_out.rearrange("n (c p) d -> n c p d", p=128)

    pab = [PS[0][:, 512 * i:512 * (i + 1)] for i in range(2)]
    pttb = [PS[0][:, 1024 + 512 * i:1536 + 512 * i].bitcast(BF16) for i in range(2)]
    py = [PS[1][:, 512 * i:512 * (i + 1)] for i in range(2)]

    uc_rd = [None] * 4
    ub_rd = None
    uT_rd = None
    alf_rd = None; buf_rd = None; hsf_rd = []
    hsb_rd = {}
    xcc_rd = [None, None]
    xof_rd = []
    xlnf_rd = None
    pab_ev = [None, None]
    py_ev = [None, None]
    scan_m = None

    carry_m = [None] * 8
    for n in range(NB):
        P.wait_all("vector", carry_m)
        cz = P.op("vector", lambda h: h.memset(carry[:], 0.0))
        for tg in range(8):
            for li in range(4):
                lc = 4 * tg + li
                mx = stA.dma(uc[li][:], xinr[n, lc], wait=[uc_rd[li]])
                P.wait_all("scalar", [mx, ub_rd])
                cb = P.op("scalar", lambda h, li=li: h.copy(out=ub[:], in_=uc[li][:]))
                for grp in range(2):
                    pt = pttb[grp]
                    P.wait_all("tensor", [cb, pab_ev[0], pab_ev[1]])
                    for a in range(4):
                        kt = 4 * grp + a
                        tm = P.op("tensor", lambda h, kt=kt, a=a, pt=pt: h.transpose(
                            pt[:, 128 * a:128 * (a + 1)], ub[:, 128 * kt:128 * (kt + 1)], C["identb"][:]))
                    P.wait("scalar", tm)
                    P.wait_all("scalar", [uT_rd])
                    cm = P.op("scalar", lambda h, grp=grp, li=li, pt=pt: h.copy(
                        out=uTv[:, 4 * grp:4 * grp + 4, 128 * li:128 * (li + 1)],
                        in_=pt[:, 0:512].rearrange("p (a t) -> p a t", t=128)))
                    P.wait("tensor", cm)
                ub_rd = cm
            tdone = P.mark("scalar")
            for ncb in range(8):
                for which, (wt, bcol, dst) in enumerate(((watb, bacol, alf), (wbtb, bbcol, buf_))):
                    pb = pab[which]
                    P.wait_all("tensor", [tdone, pab_ev[which]])
                    for kt in range(8):
                        P.op("tensor", lambda h, kt=kt, ncb=ncb, wt=wt, pb=pb: h.matmul(
                            pb[:], lhsT=wt[:, 1024 * kt + 128 * ncb:1024 * kt + 128 * (ncb + 1)],
                            rhs=uTv[:, kt, :], start=(kt == 0), stop=(kt == 7)))
                    mm = P.mark("tensor")
                    P.wait_all("scalar", [mm, alf_rd if which == 0 else buf_rd])
                    ev = P.op("scalar", lambda h, which=which, ncb=ncb, bcol=bcol, dst=dst, pb=pb: h.activation(
                        out=dst[:], in_=pb[:], func=(AF.Sigmoid if which == 0 else AF.Identity),
                        bias=bcol[:, ncb:ncb + 1], scale=1.0))
                    pab_ev[which] = ev
                    if which == 0:
                        alf_rd = ev
                    else:
                        buf_rd = ev
                P.wait_all("vector", [alf_rd, buf_rd, cz, carry_m[ncb]] + (hsf_rd or []))
                sc = P.op("vector", lambda h, ncb=ncb: h.tensor_tensor_scan(
                    out=hsf[:], data0=alf[:], data1=buf_[:], initial=carry[:, ncb:ncb + 1],
                    op0=ALU.mult, op1=ALU.add))
                P.wait("vector", sc)
                cu = P.op("vector", lambda h, ncb=ncb: h.tensor_copy(out=carry[:, ncb:ncb + 1],
                                                                     in_=hsf[:, 511:512]))
                carry_m[ncb] = cu
                alf_rd = sc
                buf_rd = sc
                P.wait_all("scalar", [sc, hsb_rd.get(ncb)])
                hc = P.op("scalar", lambda h, ncb=ncb: h.copy(out=hsbv[:, ncb, :], in_=hsf[:]))
                hsf_rd = [hc, cu]
                hsb_rd[ncb] = hc
            scan_done = P.mark("scalar")
            for tc in range(4):
                lc = 4 * tg + tc
                mxc = stC.dma(xcc[tc % 2][:], xcurr[n, lc], wait=[xcc_rd[tc % 2]])
                for cc in range(2):
                    P.wait_all("tensor", [scan_done, py_ev[cc]])
                    for ncb in range(8):
                        P.op("tensor", lambda h, ncb=ncb, tc=tc, cc=cc: h.matmul(
                            py[cc][:], lhsT=hsbv[:, ncb, 128 * tc:128 * (tc + 1)],
                            rhs=cwtb[:, 1024 * ncb + 512 * cc:1024 * ncb + 512 * (cc + 1)],
                            start=(ncb == 0), stop=(ncb == 7)))
                mmy = P.mark("tensor")
                P.wait_all("vector", [mmy, mxc] + xof_rd)
                for cc in range(2):
                    v1 = P.op("vector", lambda h, cc=cc, tc=tc: h.tensor_tensor(
                        out=xof[:, 512 * cc:512 * (cc + 1)], in0=uc[tc][:, 512 * cc:512 * (cc + 1)],
                        in1=dbc[:, 512 * cc:512 * (cc + 1)], op=ALU.mult))
                    P.wait("vector", v1)
                    v2 = P.op("vector", lambda h, cc=cc: h.tensor_tensor(
                        out=xof[:, 512 * cc:512 * (cc + 1)], in0=xof[:, 512 * cc:512 * (cc + 1)],
                        in1=py[cc][:], op=ALU.add))
                    P.wait("vector", v2)
                    v3 = P.op("vector", lambda h, cc=cc, tc=tc: h.tensor_tensor(
                        out=xof[:, 512 * cc:512 * (cc + 1)], in0=xof[:, 512 * cc:512 * (cc + 1)],
                        in1=xcc[tc % 2][:, 512 * cc:512 * (cc + 1)], op=ALU.add))
                    py_ev[cc] = v3
                uc_rd[tc] = v3
                xcc_rd[tc % 2] = v3
                mo = stH.dma(xoutr[n, lc], xof[:], wait=[v3])
                xof_rd = [mo]
                P.wait_all("scalar", [xlnf_rd])
                lnm = ln.run(xof, [xlnf], dep=v3)
                xof_rd.append(lnm)
                m2 = stI.dma(lnoutr[n, lc], xlnf[:], wait=[lnm])
                xlnf_rd = m2
            for ncb in range(8):
                hsb_rd[ncb] = P.mark("tensor")
            uT_rd = P.mark("tensor")
    P.barrier()


EXT_IN = {
    "x0": ("L0_attn_a_X_embed", [NB, T, D]),
    "aw_q": ("L0_attn_a_W_q", [1024, 1024]), "ab_q": ("L0_attn_a_b_q", [1024]),
    "aw_k": ("L0_attn_a_W_k", [1024, 1024]), "ab_k": ("L0_attn_a_b_k", [1024]),
    "aw_v": ("L0_attn_a_W_v", [1024, 1024]), "ab_v": ("L0_attn_a_b_v", [1024]),
    "aw_o": ("L0_attn_a_W_o", [1024, 1024]), "ab_o": ("L0_attn_a_b_o", [1024]),
    "bw_a": ("L0_attn_b_W_alpha_w", [1024, 1024]), "bb_a": ("L0_attn_b_W_alpha_b", [1024]),
    "bw_b": ("L0_attn_b_W_B_w", [1024, 1024]), "bb_b": ("L0_attn_b_W_B_b", [1024]),
    "bw_c": ("L0_attn_b_C_w", [1024, 1024]), "bd": ("L0_attn_b_D", [1024]),
    "f_wqkv": ("L0_ffn_wqkv", [2, 1024, 1536]),
    "f_qs": ("L0_ffn_q_scale", [2, 64]), "f_ks": ("L0_ffn_k_scale", [2, 64]),
    "f_wo": ("L0_ffn_wo", [2, 512, 1024]),
    "f_lng": ("L0_ffn_ln_g", [2, 1024]), "f_lnb": ("L0_ffn_ln_b", [2, 1024]),
    "f_w1": ("L0_ffn_w1", [2, 1024, 5460]), "f_b1": ("L0_ffn_b1", [2, 5460]),
    "f_w2": ("L0_ffn_w2", [2, 2730, 1024]), "f_b2": ("L0_ffn_b2", [2, 1024]),
    "c_ident": ("c_ident", [128, 128]), "c_identbf": ("c_identbf", [128, 128]),
    "c_rot": ("c_rot", [T, 256]), "c_mfirst": ("c_mfirst", [128, 128]),
    "c_mlast": ("c_mlast", [128, 128]),
}
SCRATCH = {
    "xcur": [NB, T, D], "xln": [NB, T, D], "xf0": [NB, T, D], "xf1": [NB, T, D],
}

class Ctx:
    def __init__(self, nc):
        self.nc = nc
        self.dram = {}

    def din(self, key):
        if key not in self.dram:
            name, shape = EXT_IN[key]
            dt = BF16 if key == "c_identbf" else F32
            self.dram[key] = self.nc.dram_tensor(name, shape, dt, kind="ExternalInput").ap()
        return self.dram[key]

    def dscr(self, key, kind="Internal"):
        if key not in self.dram:
            if key == "OUT":
                self.dram[key] = self.nc.dram_tensor("OUT", [NB, T, D], F32, kind="ExternalOutput").ap()
            elif key == "vsp":
                self.dram[key] = self.nc.dram_tensor("vsp", [NB, T, D], BF16, kind=kind).ap()
            elif key == "diagH":
                self.dram[key] = self.nc.dram_tensor("diagH", [NB, 16, T], F32, kind=kind).ap()
            else:
                self.dram[key] = self.nc.dram_tensor(key, SCRATCH[key], F32, kind=kind).ap()
        return self.dram[key]


def emit_pass(name, nc, P, gstk, cx, C, scratch_kind="Internal"):
    with ExitStack() as stk:
        if name == "p0":
            io = {"x0": cx.din("x0"), "wq": cx.din("aw_q"), "wk": cx.din("aw_k"),
                  "wv": cx.din("aw_v"), "wo": cx.din("aw_o"), "bq": cx.din("ab_q"),
                  "bk": cx.din("ab_k"), "bv": cx.din("ab_v"), "bo": cx.din("ab_o"),
                  "xcur": cx.dscr("xcur", scratch_kind), "xlnout": cx.dscr("xln", scratch_kind),
                  "vsp": cx.dscr("vsp"), "diagH": cx.dscr("diagH")}
            emit_p0(nc, P, stk, io, C)
        elif name.startswith("fa"):
            call, l = int(name[2]), int(name[3])
            xi = {"10": "xln", "11": "xf1", "20": "xln", "21": "xf1"}[name[2:4]]
            xo = "xf0" if l == 0 else "xf0"
            xo = "xf0"
            io = {"wqkv": cx.din("f_wqkv"), "wo": cx.din("f_wo"),
                  "q_scale": cx.din("f_qs"), "k_scale": cx.din("f_ks"),
                  "c_rot": cx.din("c_rot")}
            emit_ffn_attn(nc, P, stk, io, C, l, cx.dscr(xi, scratch_kind), cx.dscr(xo, scratch_kind),
                          tag=name)
        elif name.startswith("fm"):
            call, l = int(name[2]), int(name[3])
            io = {"w1": cx.din("f_w1"), "b1": cx.din("f_b1"), "w2": cx.din("f_w2"),
                  "b2": cx.din("f_b2"), "ln_g": cx.din("f_lng"), "ln_b": cx.din("f_lnb")}
            xin = cx.dscr("xf0", scratch_kind)
            if l == 0:
                emit_ffn_mlp(nc, P, stk, io, C, l, xin, cx.dscr("xf1", scratch_kind), tag=name)
            else:
                if call == 1:
                    emit_ffn_mlp(nc, P, stk, io, C, l, xin, cx.dscr("xcur", scratch_kind),
                                 extra_res=cx.dscr("xcur", scratch_kind),
                                 ln_out=cx.dscr("xln", scratch_kind), tag=name)
                else:
                    emit_ffn_mlp(nc, P, stk, io, C, l, xin, cx.dscr("OUT"),
                                 extra_res=cx.dscr("xcur", scratch_kind), tag=name)
        elif name == "p2":
            io = {"W_alpha_w": cx.din("bw_a"), "W_alpha_b": cx.din("bb_a"),
                  "W_B_w": cx.din("bw_b"), "W_B_b": cx.din("bb_b"),
                  "C_w": cx.din("bw_c"), "Dv": cx.din("bd")}
            emit_attn_b(nc, P, stk, io, C, cx.dscr("xln", scratch_kind),
                        cx.dscr("xcur", scratch_kind), cx.dscr("xcur", scratch_kind),
                        cx.dscr("xln", scratch_kind), tag="p2")
        else:
            raise ValueError(name)

PASS_SEQ = ["p0", "fa10", "fm10", "fa11", "fm11", "p2", "fa20", "fm20", "fa21", "fm21"]
PASS_IO = {
    "p0":   (["x0"], ["xcur", "xln"]),
    "fa10": (["xln"], ["xf0"]), "fm10": (["xf0"], ["xf1"]),
    "fa11": (["xf1"], ["xf0"]), "fm11": (["xf0", "xcur"], ["xcur", "xln"]),
    "p2":   (["xln", "xcur"], ["xcur", "xln"]),
    "fa20": (["xln"], ["xf0"]), "fm20": (["xf0"], ["xf1"]),
    "fa21": (["xf1"], ["xf0"]), "fm21": (["xf0", "xcur"], ["OUT"]),
}

def build_program(passes, fused):
    nc = bass.Bass("TRN2", target_bir_lowering=False)
    cx = Ctx(nc)
    gstk = ExitStack()
    C = {}
    C["identf"] = gstk.enter_context(nc.sbuf_tensor("identf", [128, 128], F32))
    C["identb"] = gstk.enter_context(nc.sbuf_tensor("identb", [128, 128], BF16))
    C["mfirst"] = gstk.enter_context(nc.sbuf_tensor("mfirst", [128, 128], F32))
    C["mlast"] = gstk.enter_context(nc.sbuf_tensor("mlast", [128, 128], F32))
    C["PS"] = [gstk.enter_context(nc.psum_tensor(f"PS{i}", [128, 2048], F32)) for i in range(2)]
    P = Prog(nc, gstk)
    ST = {"A": P.stream("sync", "sA", 3), "B": P.stream("sync", "sB", 2),
          "C": P.stream("sync", "sC", 2), "D": P.stream("sync", "sD", 2),
          "E": P.stream("sync", "sE", 2), "F": P.stream("gpsimd", "sF", 2),
          "G": P.stream("gpsimd", "sG", 2), "H": P.stream("gpsimd", "sH", 2),
          "I": P.stream("gpsimd", "sI", 2)}
    C["ST"] = ST
    cl = [ST["B"].dma(C["identf"][:], cx.din("c_ident")),
          ST["B"].dma(C["identb"][:], cx.din("c_identbf")),
          ST["B"].dma(C["mfirst"][:], cx.din("c_mfirst")),
          ST["B"].dma(C["mlast"][:], cx.din("c_mlast"))]
    block = gstk.enter_context(nc.Block())
    for e in ("tensor", "vector", "scalar", "gpsimd"):
        P.wait_all(e, cl)
    for i, name in enumerate(passes):
        emit_pass(name, nc, P, gstk, cx, C, scratch_kind="Internal")
    P.barrier()
    P.emit(block)
    gstk.close()
    return nc, cx



_CONSTS = host_constants()

def kernel(**inputs):
    x = np.ascontiguousarray(inputs["L0_attn_a_X_embed"])
    nc, cx = build_program(PASS_SEQ, fused=True)
    in_maps = []
    for c in range(8):
        im = {"L0_attn_a_X_embed": x[2 * c:2 * c + 2]}
        for key, (name, shape) in EXT_IN.items():
            if name == "L0_attn_a_X_embed" or name.startswith("c_"):
                continue
            im[name] = np.ascontiguousarray(inputs[name])
        im["c_ident"] = _CONSTS["c_ident"]
        im["c_identbf"] = _CONSTS["c_identbf"]
        im["c_rot"] = _CONSTS["c_rot"]
        im["c_mfirst"] = _CONSTS["c_mfirst"]
        im["c_mlast"] = _CONSTS["c_mlast"]
        in_maps.append(im)
    res = run_bass_kernel_spmd(nc, in_maps, core_ids=list(range(8)))
    out = np.concatenate([res.results[c]["OUT"] for c in range(8)], axis=0)
    return out.astype(np.float32)
